# version 02 (final)
# baseline (speedup 1.0000x reference)
import math
import numpy as np
from contextlib import ExitStack

import concourse.bass as bass
import concourse.mybir as mybir
from concourse.bass_utils import run_bass_kernel_spmd

F32 = mybir.dt.float32
AF = mybir.ActivationFunctionType
ALU = mybir.AluOpType
AX = mybir.AxisListType

T = 4096
D = 1024
NH = 16
DK = 64
TT = 512
NTT = T // TT
NCD = D // 128
NEG = -1.0e9


class Sem:
    def __init__(self, nc, name):
        self.h = nc.alloc_semaphore(name)
        self.n = 0
        self.name = name

    def inc(self, ins, k=1):
        self.n += k
        ins.then_inc(self.h, k)
        return (self, self.n)

    def tok(self):
        return (self, self.n)


class PB:

    def __init__(self, nc):
        self.nc = nc
        self.waited = {}
        self.sems = {}

    def sem(self, name):
        if name not in self.sems:
            self.sems[name] = Sem(self.nc, name)
        return self.sems[name]

    def wait(self, eng, engname, tokens):
        if isinstance(tokens, tuple) and len(tokens) == 2 and isinstance(tokens[0], Sem):
            tokens = [tokens]
        for sem, v in tokens:
            if v <= 0:
                continue
            key = (engname, sem.name)
            if self.waited.get(key, -1) >= v:
                continue
            eng.wait_ge(sem.h, v)
            self.waited[key] = v



def host_constants():
    ident = np.eye(128, dtype=np.float32)
    ql = np.arange(128)[:, None]
    kl = np.arange(128)[None, :]
    mask_tri = np.where(ql >= kl, 0.0, NEG).astype(np.float32)
    mask_far = np.where(ql <= kl, 0.0, NEG).astype(np.float32)
    inv_freq = 1.0 / (10000.0 ** (np.arange(0, DK, 2, dtype=np.float32) / DK))
    freqs = np.outer(np.arange(T, dtype=np.float32), inv_freq)
    cosrep = np.tile(np.cos(freqs), (1, NH)).astype(np.float32)
    sinrep = np.tile(np.sin(freqs), (1, NH)).astype(np.float32)
    return dict(ident=ident, mask_tri=mask_tri, mask_far=mask_far,
                cosrep=cosrep, sinrep=sinrep)


BAND_DELTAS = [-512, -384, -256, -128, 0, 128]

def host_band_tiles(rel_bias):
    kl = np.arange(128)[:, None]
    ql = np.arange(TT)[None, :]
    out = np.empty((NH, len(BAND_DELTAS), 128, TT), dtype=np.float32)
    for d, delta in enumerate(BAND_DELTAS):
        idx = np.clip(delta + ql - kl, -128, 128) + 128
        out[:, d] = rel_bias[:, idx]
    return np.ascontiguousarray(out)



def emit_l0a(pb, block, io, ctx, rms=False, rope=False):
    nc = pb.nc
    sb = lambda n, sh: ctx.enter_context(nc.sbuf_tensor(n, sh, F32))

    nxpar = 1 if rope else 2
    xt = sb("a_xt", [128, nxpar * 4 * D])
    sq = sb("a_sq", [128, D])
    stt_ = sb("a_st", [128, 64])
    xn_tok = sb("a_xntok", [128, D])
    xn_fm = sb("a_xnfm", [128, NCD * TT])
    wq = sb("a_wq", [128, 64 * 128])
    wk = sb("a_wk", [128, 64 * 128])
    wv = sb("a_wv", [128, 64 * 128])
    qst = sb("a_qst", [128, 2 * TT])
    kst = sb("a_kst", [128, 2 * TT])
    vst = sb("a_vst", [128, 2 * 1056])
    qq = sb("a_qq", [128, TT])
    nacc = sb("a_nacc", [1, 32])
    ones = sb("a_ones", [128, 128])
    sid = sb("a_sid", [128, 128])
    biasq = sb("a_biasq", [128, 8])
    biask = sb("a_biask", [128, 8])
    bvrow = sb("a_bvrow", [1, D])
    srb = sb("a_srb", [16, 257])
    small = sb("a_small", [16, 8])
    rowbuf = sb("a_rowbuf", [1, 128])
    cvec = sb("a_cvec", [128, 48])
    cst = sb("a_cst", [128, 8])
    if rope:
        cosf = sb("a_cosf", [128, T])
        sinf = sb("a_sinf", [128, T])
        rsc = sb("a_rsc", [128, TT])
        rsc2 = sb("a_rsc2", [128, TT])
        swp = sb("a_swp", [128, 128])

    psS0 = ctx.enter_context(nc.psum_tensor("a_ps0", [128, 1536], F32))
    psS1 = ctx.enter_context(nc.psum_tensor("a_ps1", [128, 1536], F32))
    psO = ctx.enter_context(nc.psum_tensor("a_psO", [128, 512], F32))
    psX = ctx.enter_context(nc.psum_tensor("a_psX", [128, 512], F32))

    s_in = [pb.sem("a_in0"), pb.sem("a_in1")]
    s_w = pb.sem("a_w")
    s_pe = pb.sem("a_pe")
    s_act = pb.sem("a_act")
    s_dve = pb.sem("a_dve")
    s_pool = pb.sem("a_pool")
    s_oq = [pb.sem("a_oq0"), pb.sem("a_oq1")]
    s_ok = [pb.sem("a_ok0"), pb.sem("a_ok1")]
    s_ov = [pb.sem("a_ov0"), pb.sem("a_ov1")]
    s_ocv = pb.sem("a_ocv")

    W4 = lambda w: w.rearrange("(di p) (do j) -> p di do j", p=128, j=128)
    W4o = lambda b: b.rearrange("p (di do j) -> p di do j", di=8, j=128)

    @block.sync
    def _(sync):
        s_w.inc(sync.dma_start(out=W4o(wq[:, :]), in_=W4(io["l0qT"])), 16)
        s_w.inc(sync.dma_start(out=W4o(wk[:, :]), in_=W4(io["l0kT"])), 16)
        s_w.inc(sync.dma_start(out=W4o(wv[:, :]), in_=W4(io["l0vT"])), 16)
        s_w.inc(sync.dma_start(out=biasq[:, :], in_=io["bq_l"]), 16)
        s_w.inc(sync.dma_start(out=biask[:, :], in_=io["bk_l"]), 16)
        s_w.inc(sync.dma_start(out=bvrow[:, :], in_=io["bv_row"]), 16)
        s_w.inc(sync.dma_start(out=srb[:, :], in_=io["rel_bias"]), 16)
        s_w.inc(sync.dma_start(out=sid[:, :], in_=io["ident"]), 16)
        if rope:
            s_w.inc(sync.dma_start(out=cosf[:, :], in_=io["cosfm"]), 16)
            s_w.inc(sync.dma_start(out=sinf[:, :], in_=io["sinfm"]), 16)
            s_w.inc(sync.dma_start(out=swp[:, :], in_=io["swp"]), 16)

    @block.gpsimd
    def _(g):
        pb.wait(g, "pool", s_w.tok())
        s_pool.inc(g.memset(ones[:, :], 1.0))
        s_pool.inc(g.memset(nacc[:, :], 0.0))
        for par in range(2):
            vs = vst[:, par * 1056:(par + 1) * 1056].rearrange("p (s j) -> p s j", s=16)
            s_pool.inc(g.memset(vs[:, :, 64:65], 1.0))
            s_pool.inc(g.memset(vs[:, :, 65:66], 0.0))
        s_pool.inc(g.memset(cst[:, 0:1], 1e-5))
        s_pool.inc(g.memset(cst[:, 1:2], 1e-6))
        pb.wait(g, "pool", s_pool.tok())
        s_pool.inc(g.tensor_scalar_mul(biasq[:, :], biasq[:, :], 0.125))
    tok_init = s_pool.tok()

    tok_xload = [None, None]
    tok_xfree = [[], []]
    tok_qq_pe = None

    for i in range(NTT):
        par = i % nxpar
        xv = xt[:, par * 4 * D:(par + 1) * 4 * D]

        @block.sync
        def _(sync, i=i, par=par, xv=xv):
            pb.wait(sync, "sync", tok_xfree[par])
            tok_xload[par] = s_in[par].inc(
                sync.dma_start(
                    out=xv.rearrange("p (s d) -> p s d", s=4),
                    in_=io["x"][i * TT:(i + 1) * TT, :].rearrange("(s p) d -> p s d", p=128),
                ), 16)

        for s in range(4):
            xsub = xv[:, s * D:(s + 1) * D]
            c0 = 16 * s

            @block.vector
            def _(v, xsub=xsub, c0=c0, par=par):
                pb.wait(v, "dve", tok_xload[par])
                pb.wait(v, "dve", s_dve.tok())
                pb.wait(v, "dve", s_pe.tok())
                s_dve.inc(v.reduce_sum(stt_[:, c0:c0 + 1], xsub, axis=AX.X))

            @block.scalar
            def _(sc, xsub=xsub, c0=c0, par=par):
                pb.wait(sc, "act", tok_xload[par])
                pb.wait(sc, "act", s_act.tok())
                s_act.inc(sc.activation(sq[:, :], xsub, AF.Square,
                                        accum_out=stt_[:, c0 + 1:c0 + 2]))

            @block.vector
            def _(v, c0=c0):
                pb.wait(v, "dve", s_act.tok())
                pb.wait(v, "dve", s_dve.tok())
                s_dve.inc(v.tensor_scalar_mul(stt_[:, c0 + 2:c0 + 3], stt_[:, c0:c0 + 1], 1.0 / D))
                pb.wait(v, "dve", s_dve.tok())
                s_dve.inc(v.tensor_mul(stt_[:, c0 + 3:c0 + 4], stt_[:, c0 + 2:c0 + 3], stt_[:, c0 + 2:c0 + 3]))
                pb.wait(v, "dve", s_dve.tok())
                s_dve.inc(v.scalar_tensor_tensor(stt_[:, c0 + 4:c0 + 5], stt_[:, c0 + 1:c0 + 2],
                                                 1.0 / D, stt_[:, c0 + 3:c0 + 4], ALU.mult, ALU.subtract))

            @block.scalar
            def _(sc, c0=c0):
                pb.wait(sc, "act", s_dve.tok())
                pb.wait(sc, "act", s_act.tok())
                pb.wait(sc, "act", tok_init)
                s_act.inc(sc.activation(stt_[:, c0 + 5:c0 + 6], stt_[:, c0 + 4:c0 + 5], AF.Sqrt,
                                        bias=cst[:, 0:1]))

            @block.vector
            def _(v, xsub=xsub, c0=c0):
                pb.wait(v, "dve", s_act.tok())
                s_dve.inc(v.reciprocal(stt_[:, c0 + 6:c0 + 7], stt_[:, c0 + 5:c0 + 6]))
                pb.wait(v, "dve", s_dve.tok())
                s_dve.inc(v.tensor_scalar(xn_tok[:, :], xsub, stt_[:, c0 + 2:c0 + 3],
                                          stt_[:, c0 + 6:c0 + 7], ALU.subtract, ALU.mult))

            if rms:
                @block.scalar
                def _(sc, c0=c0):
                    pb.wait(sc, "act", s_dve.tok())
                    pb.wait(sc, "act", s_act.tok())
                    s_act.inc(sc.activation(sq[:, :], xn_tok[:, :], AF.Square,
                                            accum_out=stt_[:, c0 + 7:c0 + 8]))

                @block.vector
                def _(v, c0=c0):
                    pb.wait(v, "dve", s_act.tok())
                    pb.wait(v, "dve", s_dve.tok())
                    s_dve.inc(v.tensor_scalar_mul(stt_[:, c0 + 8:c0 + 9], stt_[:, c0 + 7:c0 + 8], 1.0 / D))

                @block.scalar
                def _(sc, c0=c0):
                    pb.wait(sc, "act", s_dve.tok())
                    pb.wait(sc, "act", s_act.tok())
                    s_act.inc(sc.activation(stt_[:, c0 + 9:c0 + 10], stt_[:, c0 + 8:c0 + 9], AF.Sqrt,
                                            bias=cst[:, 1:2]))

                @block.vector
                def _(v, c0=c0):
                    pb.wait(v, "dve", s_act.tok())
                    s_dve.inc(v.reciprocal(stt_[:, c0 + 10:c0 + 11], stt_[:, c0 + 9:c0 + 10]))
                    pb.wait(v, "dve", s_dve.tok())
                    s_dve.inc(v.tensor_scalar_mul(xn_tok[:, :], xn_tok[:, :], stt_[:, c0 + 10:c0 + 11]))

            tok_xn = s_dve.tok()

            @block.tensor
            def _(t, s=s, tok_xn=tok_xn):
                pb.wait(t, "pe", tok_xn)
                pb.wait(t, "pe", tok_init)
                pb.wait(t, "pe", s_w.tok())
                pb.wait(t, "pe", s_act.tok())
                for c in range(8):
                    ins = t.transpose(psS0[:, c * 128:(c + 1) * 128],
                                      xn_tok[:, c * 128:(c + 1) * 128], sid[:, :])
                s_pe.inc(ins)
            tok_tr = s_pe.tok()

            @block.scalar
            def _(sc, s=s, tok_tr=tok_tr):
                pb.wait(sc, "act", tok_tr)
                pb.wait(sc, "act", s_pe.tok())
                for half in range(2):
                    ins = sc.activation(
                        xn_fm[:, :].rearrange("p (c t) -> p c t", c=NCD)[:, half * 4:(half + 1) * 4,
                                                                         s * 128:(s + 1) * 128],
                        psS0[:, half * 512:(half + 1) * 512].rearrange("p (c t) -> p c t", c=4),
                        AF.Identity)
                s_act.inc(ins)

        tok_xnfm = s_act.tok()
        tok_xfree[par] = [s_act.tok(), s_dve.tok()]

        for qk, (wbuf, bias, scale, stage, ssem, odram, nslot) in enumerate((
            (wq, biasq, 0.125, qst, s_oq, io["Qfm"], 0),
            (wk, biask, 1.0, kst, s_ok, io["Kfm"], 16),
        )):
            for do in range(8):
                stg = stage[:, (do % 2) * TT:(do % 2 + 1) * TT]
                pslc = psS1[:, (do % 3) * 512:(do % 3 + 1) * 512]

                @block.tensor
                def _(t, do=do, wbuf=wbuf, pslc=pslc, tok_xnfm=tok_xnfm):
                    pb.wait(t, "pe", tok_xnfm)
                    pb.wait(t, "pe", s_act.tok())
                    for di in range(8):
                        ins = t.matmul(pslc, wbuf[:, (di * 8 + do) * 128:(di * 8 + do + 1) * 128],
                                       xn_fm[:, di * TT:(di + 1) * TT],
                                       start=(di == 0), stop=(di == 7))
                    s_pe.inc(ins)
                tok_mm = s_pe.tok()

                @block.scalar
                def _(sc, do=do, stg=stg, bias=bias, scale=scale, ssem=ssem, pslc=pslc, tok_mm=tok_mm):
                    pb.wait(sc, "act", tok_mm)
                    pb.wait(sc, "act", (ssem[do % 2], ssem[do % 2].n))
                    if rope:
                        pb.wait(sc, "act", s_dve.tok())
                    s_act.inc(sc.activation(stg, pslc, AF.Identity, bias=bias[:, do:do + 1], scale=scale))
                tok_ev = s_act.tok()

                if rope:
                    @block.vector
                    def _(v, i=i, stg=stg, tok_ev=tok_ev):
                        pb.wait(v, "dve", tok_ev)
                        pb.wait(v, "dve", s_w.tok())
                        pb.wait(v, "dve", s_dve.tok())
                        s_dve.inc(v.tensor_mul(rsc[:, :], stg, cosf[:, i * TT:(i + 1) * TT]))
                    tok_t1 = s_dve.tok()

                    @block.tensor
                    def _(t, stg=stg, tok_ev=tok_ev):
                        pb.wait(t, "pe", tok_ev)
                        pb.wait(t, "pe", s_w.tok())
                        pb.wait(t, "pe", s_dve.tok())
                        ins = t.matmul(psS0[:, 1024:1536], swp[:, :], stg, start=True, stop=True)
                        s_pe.inc(ins)
                    tok_sw = s_pe.tok()

                    @block.vector
                    def _(v, i=i, stg=stg, tok_sw=tok_sw, tok_t1=tok_t1):
                        pb.wait(v, "dve", tok_sw)
                        pb.wait(v, "dve", tok_t1)
                        s_dve.inc(v.tensor_mul(rsc2[:, :], psS0[:, 1024:1536],
                                               sinf[:, i * TT:(i + 1) * TT]))
                        pb.wait(v, "dve", s_dve.tok())
                        s_dve.inc(v.tensor_add(stg, rsc[:, :], rsc2[:, :]))
                    tok_ev = s_dve.tok()

                @block.gpsimd
                def _(g, do=do, i=i, stg=stg, ssem=ssem, odram=odram, tok_ev=tok_ev):
                    pb.wait(g, "pool", tok_ev)
                    ssem[do % 2].inc(
                        g.dma_start(out=odram[do * 128:(do + 1) * 128, i * TT:(i + 1) * TT], in_=stg), 16)

                @block.scalar
                def _(sc, stg=stg, tok_ev=tok_ev):
                    pb.wait(sc, "act", tok_ev)
                    if tok_qq_pe is not None:
                        pb.wait(sc, "act", tok_qq_pe)
                    s_act.inc(sc.activation(qq[:, :], stg, AF.Square))
                tok_sq = s_act.tok()

                @block.tensor
                def _(t, tok_sq=tok_sq):
                    pb.wait(t, "pe", tok_sq)
                    pb.wait(t, "pe", s_dve.tok())
                    t.matmul(psO[0:1, 0:512], ones[0:64, 0:1], qq[0:64, :], start=True, stop=True)
                    ins = t.matmul(psX[0:1, 0:512], ones[64:128, 0:1], qq[64:128, :], start=True, stop=True)
                    s_pe.inc(ins)
                tok_qq_pe = s_pe.tok()

                @block.vector
                def _(v, do=do, nslot=nslot, tok_nm=tok_qq_pe):
                    pb.wait(v, "dve", tok_nm)
                    pb.wait(v, "dve", tok_init)
                    pb.wait(v, "dve", s_dve.tok())
                    s_dve.inc(v.reduce_max(stt_[0:1, 32:33], psO[0:1, 0:512], axis=AX.X))
                    s_dve.inc(v.reduce_max(stt_[0:1, 33:34], psX[0:1, 0:512], axis=AX.X))
                    pb.wait(v, "dve", s_dve.tok())
                    s_dve.inc(v.tensor_max(nacc[0:1, nslot + 2 * do:nslot + 2 * do + 1],
                                           nacc[0:1, nslot + 2 * do:nslot + 2 * do + 1], stt_[0:1, 32:33]))
                    pb.wait(v, "dve", s_dve.tok())
                    s_dve.inc(v.tensor_max(nacc[0:1, nslot + 2 * do + 1:nslot + 2 * do + 2],
                                           nacc[0:1, nslot + 2 * do + 1:nslot + 2 * do + 2], stt_[0:1, 33:34]))

        for s in range(4):
            par_v = (i * 4 + s) % 2
            vstg = vst[:, par_v * 1056:(par_v + 1) * 1056]
            for half in range(2):
                pslc = psS1[:, ((s * 2 + half) % 3) * 512:(((s * 2 + half) % 3) + 1) * 512]

                @block.tensor
                def _(t, s=s, half=half, pslc=pslc, tok_xnfm=tok_xnfm):
                    pb.wait(t, "pe", tok_xnfm)
                    pb.wait(t, "pe", s_act.tok())
                    for di in range(8):
                        t.matmul(pslc, xn_fm[:, di * TT + s * 128: di * TT + s * 128 + 128],
                                 wv[:, di * 1024 + half * 512: di * 1024 + (half + 1) * 512],
                                 start=(di == 0), stop=False)
                    ins = t.matmul(pslc, ones[0:1, 0:128], bvrow[0:1, half * 512:(half + 1) * 512],
                                   start=False, stop=True)
                    s_pe.inc(ins)
                tok_vm = s_pe.tok()

                @block.scalar
                def _(sc, half=half, par_v=par_v, vstg=vstg, pslc=pslc, tok_vm=tok_vm):
                    pb.wait(sc, "act", tok_vm)
                    pb.wait(sc, "act", (s_ov[par_v], s_ov[par_v].n))
                    pb.wait(sc, "act", tok_init)
                    s_act.inc(sc.activation(
                        vstg.rearrange("p (sl j) -> p sl j", sl=16)[:, half * 8:(half + 1) * 8, 0:64],
                        pslc.rearrange("p (sl j) -> p sl j", sl=8),
                        AF.Identity))
            tok_vev = s_act.tok()

            @block.gpsimd
            def _(g, s=s, i=i, par_v=par_v, vstg=vstg, tok_vev=tok_vev):
                pb.wait(g, "pool", tok_vev)
                s_ov[par_v].inc(
                    g.dma_start(out=io["Vp"][i * TT + s * 128: i * TT + (s + 1) * 128, :], in_=vstg), 16)

    @block.vector
    def _(v):
        pb.wait(v, "dve", s_w.tok())
        pb.wait(v, "dve", s_dve.tok())
        s_dve.inc(v.reduce_max(small[:, 0:1], srb[:, :], axis=AX.X))
        pb.wait(v, "dve", s_dve.tok())
        s_dve.inc(v.tensor_copy(small[:, 1:2], srb[:, 0:1]))
        pb.wait(v, "dve", s_dve.tok())
        s_dve.inc(v.tensor_copy(small[:, 2:3], srb[:, 256:257]))
    tok_small = s_dve.tok()

    @block.tensor
    def _(t):
        pb.wait(t, "pe", tok_small)
        pb.wait(t, "pe", s_dve.tok())
        for j in range(3):
            ins = t.transpose(psO[0:1, j * 16:(j + 1) * 16], small[0:16, j:j + 1], sid[0:16, 0:16])
        s_pe.inc(ins)
    tok_tr3 = s_pe.tok()

    @block.scalar
    def _(sc):
        pb.wait(sc, "act", tok_tr3)
        pb.wait(sc, "act", s_act.tok())
        s_act.inc(sc.activation(rowbuf[0:1, 0:48], psO[0:1, 0:48], AF.Identity))
        pb.wait(sc, "act", s_dve.tok())
        s_act.inc(sc.activation(rowbuf[0:1, 64:96], nacc[0:1, 0:32], AF.Sqrt))
    tok_row = s_act.tok()

    @block.vector
    def _(v):
        pb.wait(v, "dve", tok_row)
        s_dve.inc(v.tensor_mul(rowbuf[0:1, 96:112], rowbuf[0:1, 64:80], rowbuf[0:1, 80:96]))
        pb.wait(v, "dve", s_dve.tok())
        s_dve.inc(v.scalar_tensor_tensor(rowbuf[0:1, 96:112], rowbuf[0:1, 96:112], 1.02,
                                         rowbuf[0:1, 0:16], ALU.mult, ALU.add))
        pb.wait(v, "dve", s_dve.tok())
        s_dve.inc(v.tensor_scalar_mul(rowbuf[0:1, 96:112], rowbuf[0:1, 96:112], -1.0))
        pb.wait(v, "dve", s_dve.tok())
        s_dve.inc(v.tensor_add(rowbuf[0:1, 112:128], rowbuf[0:1, 96:112], rowbuf[0:1, 16:32]))
        pb.wait(v, "dve", s_dve.tok())
        s_dve.inc(v.tensor_add(rowbuf[0:1, 48:64], rowbuf[0:1, 96:112], rowbuf[0:1, 32:48]))
    tok_crow = s_dve.tok()

    @block.tensor
    def _(t):
        pb.wait(t, "pe", tok_crow)
        t.matmul(psX[:, 0:16], ones[0:1, :], rowbuf[0:1, 96:112], start=True, stop=True)
        t.matmul(psX[:, 16:32], ones[0:1, :], rowbuf[0:1, 112:128], start=True, stop=True)
        ins = t.matmul(psX[:, 32:48], ones[0:1, :], rowbuf[0:1, 48:64], start=True, stop=True)
        s_pe.inc(ins)
    tok_bc = s_pe.tok()

    @block.scalar
    def _(sc):
        pb.wait(sc, "act", tok_bc)
        s_act.inc(sc.activation(cvec[:, :], psX[:, 0:48], AF.Identity))
    tok_cvec = s_act.tok()

    @block.gpsimd
    def _(g):
        pb.wait(g, "pool", tok_cvec)
        s_ocv.inc(g.dma_start(out=io["cvec"], in_=cvec[:, :]), 16)
        g.wait_ge(s_ocv.h, 16)
        for ss in (s_oq[0], s_oq[1], s_ok[0], s_ok[1], s_ov[0], s_ov[1]):
            g.wait_ge(ss.h, ss.n)



def emit_l0b(pb, block, io, ctx, nd=6, dmin=-512, skip_far=False):
    nc = pb.nc
    sb = lambda n, sh: ctx.enter_context(nc.sbuf_tensor(n, sh, F32))

    big = sb("b_big", [128, 16384])
    vph = sb("b_vph", [128, 2 * 2080])
    bnd = sb("b_bnd", [128, 2 * nd * 512])
    pp = sb("b_pp", [128, 2 * 1536])
    ost = sb("b_ost", [128, 2 * 512])
    nsb = sb("b_nsb", [128, 2 * 512])
    rden = sb("b_rden", [1, 2 * 512])
    wo = sb("b_wo", [128, 64 * 128])
    borow = sb("b_borow", [1, D])
    cvec = sb("b_cvec", [128, 48])
    ones = sb("b_ones", [1, 128])
    x1st = sb("b_x1st", [128, 2 * 1024])
    psS = [ctx.enter_context(nc.psum_tensor("b_psS0", [128, 1536], F32)),
           ctx.enter_context(nc.psum_tensor("b_psS1", [128, 1536], F32))]
    psO = [ctx.enter_context(nc.psum_tensor("b_psO0", [128, 512], F32)),
           ctx.enter_context(nc.psum_tensor("b_psO1", [128, 512], F32))]

    s_w = pb.sem("b_w")
    s_pe = pb.sem("b_pe")
    s_act = pb.sem("b_act")
    s_dve = pb.sem("b_dve")
    s_pool = pb.sem("b_pool")
    s_h = [pb.sem("b_h0"), pb.sem("b_h1")]
    s_oo = [pb.sem("b_oo0"), pb.sem("b_oo1")]
    s_ot = [pb.sem("b_ot0"), pb.sem("b_ot1")]
    s_x1 = [pb.sem("b_x10"), pb.sem("b_x11")]

    @block.sync
    def _(sync):
        s_w.inc(sync.dma_start(
            out=wo[:, :].rearrange("p (di do j) -> p di do j", di=8, j=128),
            in_=io["l0oT"].rearrange("(di p) (do j) -> p di do j", p=128, j=128)), 16)
        s_w.inc(sync.dma_start(out=borow[:, :], in_=io["bo_row"]), 16)
        s_w.inc(sync.dma_start(out=cvec[:, :], in_=io["cvec"]), 16)

    @block.gpsimd
    def _(g):
        s_pool.inc(g.memset(ones[:, :], 1.0))
    tok_init = s_pool.tok()

    NKT = 32
    tok_hfree = [[], []]
    for h in range(NH):
        par = h % 2
        qh = big[:, par * 8192:par * 8192 + 4096]
        kh = big[:, par * 8192 + 4096:par * 8192 + 8192]
        vp = vph[:, par * 2080:(par + 1) * 2080]
        bd = bnd[:, par * nd * 512:(par + 1) * nd * 512]

        @block.sync
        def _(sync, h=h, par=par, qh=qh, kh=kh, vp=vp, bd=bd):
            pb.wait(sync, "sync", tok_hfree[par])
            s_h[par].inc(sync.dma_start(out=qh[0:64, :], in_=io["Qfm"][h * 64:(h + 1) * 64, :]), 16)
            s_h[par].inc(sync.dma_start(out=kh[0:64, :], in_=io["Kfm"][h * 64:(h + 1) * 64, :]), 16)
            s_h[par].inc(sync.dma_start(
                out=vp.rearrange("p (kt j) -> p kt j", kt=32),
                in_=io["Vp"][:, h * 66:h * 66 + 65].rearrange("(kt p) j -> p kt j", p=128)), 16)
            s_h[par].inc(sync.dma_start(
                out=bd.rearrange("p (d j) -> p d j", d=nd),
                in_=io["band"][h].rearrange("d p j -> p d j")), 16)
        tok_hload = s_h[par].tok()

        for qt in range(NTT):
            opar = (h * NTT + qt) % 2
            po = psO[opar]
            dmax = dmin + (nd - 1) * 128
            lo = max(0, (512 * qt - dmax) // 128)
            hi = min(NKT - 1, (512 * qt - dmin) // 128)
            runs = []
            if lo > 0 and not skip_far:
                runs.append((list(range(0, lo)), 32 + h, False))
            runs.append((list(range(lo, hi + 1)), 0 + h, True))
            if hi < NKT - 1 and not skip_far:
                runs.append((list(range(hi + 1, NKT)), 16 + h, False))
            batches = []
            for kts, bcol, isband in runs:
                for j in range(0, len(kts), 3):
                    batches.append((kts[j:j + 3], bcol, isband))
            first_kt = True
            bpar = 0
            for bi, (kts, bcol, isband) in enumerate(batches):
                is_last_batch = bi == len(batches) - 1
                ps = psS[bpar]
                ppb = pp[:, bpar * 1536:(bpar + 1) * 1536]

                @block.tensor
                def _(t, kts=kts, qt=qt, ps=ps, qh=qh, kh=kh, tok_hload=tok_hload, bpar=bpar):
                    pb.wait(t, "pe", tok_hload)
                    pb.wait(t, "pe", s_act.tok())
                    for j, kt in enumerate(kts):
                        ins = t.matmul(ps[:, j * 512:(j + 1) * 512],
                                       kh[0:64, kt * 128:(kt + 1) * 128],
                                       qh[0:64, qt * 512:(qt + 1) * 512],
                                       start=True, stop=True)
                    s_pe.inc(ins)
                tok_s = s_pe.tok()

                if isband:
                    @block.vector
                    def _(v, kts=kts, qt=qt, ps=ps, bd=bd, tok_s=tok_s):
                        pb.wait(v, "dve", tok_s)
                        pb.wait(v, "dve", tok_hload)
                        for j, kt in enumerate(kts):
                            d = 4 * qt - kt - dmin // 128
                            ins = v.tensor_add(ps[:, j * 512:(j + 1) * 512],
                                               ps[:, j * 512:(j + 1) * 512],
                                               bd[:, d * 512:(d + 1) * 512])
                        s_dve.inc(ins)
                    tok_badd = s_dve.tok()
                else:
                    tok_badd = None

                @block.scalar
                def _(sc, kts=kts, bcol=bcol, ps=ps, ppb=ppb, tok_s=tok_s, tok_badd=tok_badd):
                    pb.wait(sc, "act", tok_s)
                    if tok_badd is not None:
                        pb.wait(sc, "act", tok_badd)
                    pb.wait(sc, "act", s_w.tok())
                    pb.wait(sc, "act", s_pe.tok())
                    n = len(kts) * 512
                    s_act.inc(sc.activation(ppb[:, 0:n], ps[:, 0:n], AF.Exp,
                                            bias=cvec[:, bcol:bcol + 1]))
                tok_exp = s_act.tok()

                @block.tensor
                def _(t, kts=kts, ppb=ppb, po=po, vp=vp, first_kt=first_kt,
                      is_last_batch=is_last_batch, tok_exp=tok_exp):
                    pb.wait(t, "pe", tok_exp)
                    pb.wait(t, "pe", s_dve.tok())
                    for j, kt in enumerate(kts):
                        ins = t.matmul(po[0:65, :], vp[:, kt * 65:(kt + 1) * 65],
                                       ppb[:, j * 512:(j + 1) * 512],
                                       start=(first_kt and j == 0),
                                       stop=(is_last_batch and j == len(kts) - 1))
                    s_pe.inc(ins)
                first_kt = False
                bpar ^= 1
            tok_vmm = s_pe.tok()

            @block.vector
            def _(v, qt=qt, po=po, opar=opar, tok_vmm=tok_vmm):
                pb.wait(v, "dve", tok_vmm)
                pb.wait(v, "dve", s_pe.tok())
                s_dve.inc(v.reciprocal(rden[0:1, opar * 512:(opar + 1) * 512], po[64:65, :]))
            tok_rec = s_dve.tok()

            @block.tensor
            def _(t, opar=opar, tok_rec=tok_rec):
                pb.wait(t, "pe", tok_rec)
                pb.wait(t, "pe", tok_init)
                pb.wait(t, "pe", s_act.tok())
                ins = t.matmul(psS[1][0:64, 0:512], ones[0:1, 0:64],
                               rden[0:1, opar * 512:(opar + 1) * 512], start=True, stop=True)
                s_pe.inc(ins)
            tok_bc = s_pe.tok()

            @block.scalar
            def _(sc, opar=opar, tok_bc=tok_bc):
                pb.wait(sc, "act", tok_bc)
                pb.wait(sc, "act", s_dve.tok())
                s_act.inc(sc.activation(nsb[0:64, opar * 512:(opar + 1) * 512],
                                        psS[1][0:64, 0:512], AF.Identity))
            tok_nsb = s_act.tok()

            @block.vector
            def _(v, qt=qt, po=po, opar=opar, h=h, tok_nsb=tok_nsb):
                pb.wait(v, "dve", tok_nsb)
                pb.wait(v, "dve", (s_oo[opar], s_oo[opar].n))
                s_dve.inc(v.tensor_mul(ost[0:64, opar * 512:(opar + 1) * 512],
                                       po[0:64, :], nsb[0:64, opar * 512:(opar + 1) * 512]))
            tok_on = s_dve.tok()

            @block.gpsimd
            def _(g, qt=qt, h=h, opar=opar, tok_on=tok_on):
                pb.wait(g, "pool", tok_on)
                s_oo[opar].inc(g.dma_start(
                    out=io["Ofm"][h * 64:(h + 1) * 64, qt * 512:(qt + 1) * 512],
                    in_=ost[0:64, opar * 512:(opar + 1) * 512]), 16)

        tok_hfree[par] = [s_pe.tok(), s_dve.tok()]

    tok_opfree = [[], []]
    for qt in range(NTT):
        par = qt % 2
        ob = big[:, par * 4096:(par + 1) * 4096]
        xb = big[:, 8192 + par * 4096: 8192 + (par + 1) * 4096]

        @block.sync
        def _(sync, qt=qt, par=par, ob=ob, xb=xb):
            if qt < 2:
                pb.wait(sync, "sync", [s_pe.tok(), s_dve.tok(),
                                       s_oo[0].tok(), s_oo[1].tok()])
            else:
                pb.wait(sync, "sync", tok_opfree[par])
            s_ot[par].inc(sync.dma_start(
                out=ob.rearrange("p (di q) -> p di q", di=8),
                in_=io["Ofm"][:, qt * 512:(qt + 1) * 512].rearrange("(di p) q -> p di q", p=128)), 16)
            s_ot[par].inc(sync.dma_start(
                out=xb.rearrange("p (s d) -> p s d", s=4),
                in_=io["x"][qt * 512:(qt + 1) * 512, :].rearrange("(s p) d -> p s d", p=128)), 16)
        tok_ol = s_ot[par].tok()

        for s in range(4):
            for half in range(2):
                ps = psS[0][:, (s % 2) * 512 + half * 0:][:, 0:512] if False else                      psS[s % 2][:, half * 512:(half + 1) * 512]

                @block.tensor
                def _(t, s=s, half=half, ps=ps, ob=ob, tok_ol=tok_ol):
                    pb.wait(t, "pe", tok_ol)
                    pb.wait(t, "pe", s_w.tok())
                    pb.wait(t, "pe", s_dve.tok())
                    for di in range(8):
                        t.matmul(ps, ob[:, di * 512 + s * 128: di * 512 + s * 128 + 128],
                                 wo[:, di * 1024 + half * 512: di * 1024 + (half + 1) * 512],
                                 start=(di == 0), stop=False)
                    ins = t.matmul(ps, ones[0:1, 0:128], borow[0:1, half * 512:(half + 1) * 512],
                                   start=False, stop=True)
                    s_pe.inc(ins)
                tok_y = s_pe.tok()

                @block.vector
                def _(v, s=s, half=half, ps=ps, xb=xb, qt=qt, tok_y=tok_y):
                    pb.wait(v, "dve", tok_y)
                    pb.wait(v, "dve", (s_x1[(qt * 4 + s) % 2], s_x1[(qt * 4 + s) % 2].n))
                    s_dve.inc(v.tensor_add(
                        x1st[:, ((qt * 4 + s) % 2) * 1024 + half * 512:][:, 0:512],
                        ps, xb[:, s * 1024 + half * 512: s * 1024 + (half + 1) * 512]))
            tok_x1 = s_dve.tok()

            @block.gpsimd
            def _(g, qt=qt, s=s, tok_x1=tok_x1):
                pb.wait(g, "pool", tok_x1)
                sp = (qt * 4 + s) % 2
                s_x1[sp].inc(g.dma_start(
                    out=io["x1"][qt * 512 + s * 128: qt * 512 + (s + 1) * 128, :],
                    in_=x1st[:, sp * 1024:(sp + 1) * 1024]), 16)

        tok_opfree[par] = [s_pe.tok(), s_dve.tok()]

    @block.gpsimd
    def _(g):
        for ss in (s_x1[0], s_x1[1], s_oo[0], s_oo[1]):
            g.wait_ge(ss.h, ss.n)


def build_l0b_program():
    nc = bass.Bass("TRN2", target_bir_lowering=False)
    io = {}
    io["x"] = nc.dram_tensor("x", [T, D], F32, kind="ExternalInput").ap()
    io["Qfm"] = nc.dram_tensor("Qfm", [D, T], F32, kind="ExternalInput").ap()
    io["Kfm"] = nc.dram_tensor("Kfm", [D, T], F32, kind="ExternalInput").ap()
    io["Vp"] = nc.dram_tensor("Vp", [T, 1056], F32, kind="ExternalInput").ap()
    io["cvec"] = nc.dram_tensor("cvec", [128, 48], F32, kind="ExternalInput").ap()
    io["band"] = nc.dram_tensor("band", [NH, 6, 128, TT], F32, kind="ExternalInput").ap()
    io["l0oT"] = nc.dram_tensor("l0oT", [D, D], F32, kind="ExternalInput").ap()
    io["bo_row"] = nc.dram_tensor("bo_row", [1, D], F32, kind="ExternalInput").ap()
    io["Ofm"] = nc.dram_tensor("Ofm", [D, T], F32, kind="Internal").ap()
    io["x1"] = nc.dram_tensor("x1", [T, D], F32, kind="ExternalOutput").ap()
    pb = PB(nc)
    with ExitStack() as ctx:
        block = ctx.enter_context(nc.Block())
        emit_l0b(pb, block, io, ctx)
    return nc


def l0b_inputs(x_b, z):
    C = host_constants()
    return {
        "x": np.ascontiguousarray(x_b),
        "band": host_band_tiles(z["L0_attn_rel_bias"]),
        "l0oT": np.ascontiguousarray(z["L0_attn_Wo"].T),
        "bo_row": np.ascontiguousarray(z["L0_attn_bo"].reshape(1, D)),
    }



def emit_ln_tile(pb, block, E, xv, tok_xload, rms=False):
    sq, stt_, xn_tok, xn_fm = E["sq"], E["stt_"], E["xn_tok"], E["xn_fm"]
    sid, cst, psT = E["sid"], E["cst"], E["psT"]
    s_act, s_dve, s_pe = E["s_act"], E["s_dve"], E["s_pe"]
    tok_init = E["tok_init"]

    for s in range(4):
        xsub = xv[:, s * D:(s + 1) * D]
        c0 = 16 * s

        @block.vector
        def _(v, xsub=xsub, c0=c0):
            pb.wait(v, "dve", tok_xload)
            pb.wait(v, "dve", s_dve.tok())
            pb.wait(v, "dve", s_pe.tok())
            s_dve.inc(v.reduce_sum(stt_[:, c0:c0 + 1], xsub, axis=AX.X))

        @block.scalar
        def _(sc, xsub=xsub, c0=c0):
            pb.wait(sc, "act", tok_xload)
            pb.wait(sc, "act", s_act.tok())
            s_act.inc(sc.activation(sq[:, :], xsub, AF.Square,
                                    accum_out=stt_[:, c0 + 1:c0 + 2]))

        @block.vector
        def _(v, c0=c0):
            pb.wait(v, "dve", s_act.tok())
            pb.wait(v, "dve", s_dve.tok())
            s_dve.inc(v.tensor_scalar_mul(stt_[:, c0 + 2:c0 + 3], stt_[:, c0:c0 + 1], 1.0 / D))
            pb.wait(v, "dve", s_dve.tok())
            s_dve.inc(v.tensor_mul(stt_[:, c0 + 3:c0 + 4], stt_[:, c0 + 2:c0 + 3], stt_[:, c0 + 2:c0 + 3]))
            pb.wait(v, "dve", s_dve.tok())
            s_dve.inc(v.scalar_tensor_tensor(stt_[:, c0 + 4:c0 + 5], stt_[:, c0 + 1:c0 + 2],
                                             1.0 / D, stt_[:, c0 + 3:c0 + 4], ALU.mult, ALU.subtract))

        @block.scalar
        def _(sc, c0=c0):
            pb.wait(sc, "act", s_dve.tok())
            pb.wait(sc, "act", s_act.tok())
            pb.wait(sc, "act", tok_init)
            s_act.inc(sc.activation(stt_[:, c0 + 5:c0 + 6], stt_[:, c0 + 4:c0 + 5], AF.Sqrt,
                                    bias=cst[:, 0:1]))

        @block.vector
        def _(v, xsub=xsub, c0=c0):
            pb.wait(v, "dve", s_act.tok())
            s_dve.inc(v.reciprocal(stt_[:, c0 + 6:c0 + 7], stt_[:, c0 + 5:c0 + 6]))
            pb.wait(v, "dve", s_dve.tok())
            s_dve.inc(v.tensor_scalar(xn_tok[:, :], xsub, stt_[:, c0 + 2:c0 + 3],
                                      stt_[:, c0 + 6:c0 + 7], ALU.subtract, ALU.mult))

        if rms:
            @block.scalar
            def _(sc, c0=c0):
                pb.wait(sc, "act", s_dve.tok())
                pb.wait(sc, "act", s_act.tok())
                s_act.inc(sc.activation(sq[:, :], xn_tok[:, :], AF.Square,
                                        accum_out=stt_[:, c0 + 7:c0 + 8]))

            @block.vector
            def _(v, c0=c0):
                pb.wait(v, "dve", s_act.tok())
                pb.wait(v, "dve", s_dve.tok())
                s_dve.inc(v.tensor_scalar_mul(stt_[:, c0 + 8:c0 + 9], stt_[:, c0 + 7:c0 + 8], 1.0 / D))

            @block.scalar
            def _(sc, c0=c0):
                pb.wait(sc, "act", s_dve.tok())
                pb.wait(sc, "act", s_act.tok())
                s_act.inc(sc.activation(stt_[:, c0 + 9:c0 + 10], stt_[:, c0 + 8:c0 + 9], AF.Sqrt,
                                        bias=cst[:, 1:2]))

            @block.vector
            def _(v, c0=c0):
                pb.wait(v, "dve", s_act.tok())
                s_dve.inc(v.reciprocal(stt_[:, c0 + 10:c0 + 11], stt_[:, c0 + 9:c0 + 10]))
                pb.wait(v, "dve", s_dve.tok())
                s_dve.inc(v.tensor_scalar_mul(xn_tok[:, :], xn_tok[:, :], stt_[:, c0 + 10:c0 + 11]))

        tok_xn = s_dve.tok()

        @block.tensor
        def _(t, s=s, tok_xn=tok_xn):
            pb.wait(t, "pe", tok_xn)
            pb.wait(t, "pe", tok_init)
            pb.wait(t, "pe", s_act.tok())
            for c in range(8):
                ins = t.transpose(psT[:, c * 128:(c + 1) * 128], xn_tok[:, c * 128:(c + 1) * 128], sid[:, :])
            E["s_pe"].inc(ins)
        tok_tr = s_pe.tok()

        @block.scalar
        def _(sc, s=s, tok_tr=tok_tr):
            pb.wait(sc, "act", tok_tr)
            pb.wait(sc, "act", s_pe.tok())
            pb.wait(sc, "act", E.get("xnfm_wait", []))
            for half in range(2):
                ins = sc.activation(
                    xn_fm[:, :].rearrange("p (c t) -> p c t", c=NCD)[:, half * 4:(half + 1) * 4,
                                                                     s * 128:(s + 1) * 128],
                    psT[:, half * 512:(half + 1) * 512].rearrange("p (c t) -> p c t", c=4),
                    AF.Identity)
            s_act.inc(ins)

    return s_act.tok()


def emit_moe(pb, block, io, ctx):
    nc = pb.nc
    sb = lambda n, sh: ctx.enter_context(nc.sbuf_tensor(n, sh, F32))

    xt = sb("m_xt", [128, 2 * 4 * D])
    sq = sb("m_sq", [128, D])
    stt_ = sb("m_st", [128, 64])
    xn_tok = sb("m_xntok", [128, D])
    xn_fm = sb("m_xnfm", [128, NCD * TT])
    wb1 = sb("m_wb1", [128, 2 * 1024])
    wb2 = sb("m_wb2", [128, 2 * 1024])
    hb = sb("m_hb", [128, 16 * TT])
    oacc = sb("m_oacc", [128, 4 * D])
    x2st = sb("m_x2st", [128, 2 * D])
    sgate = sb("m_sgate", [128, 64])
    b1l = sb("m_b1l", [128, 128])
    b2all = sb("m_b2all", [8, 1024])
    gex = sb("m_gex", [128, 24])
    gst = sb("m_gst", [128, 8])
    gcol = sb("m_gcol", [128, 32])
    gT = sb("m_gT", [8, 512])
    ones = sb("m_ones", [1, 128])
    sid = sb("m_sid", [128, 128])
    cst = sb("m_cst", [128, 8])

    psS0 = ctx.enter_context(nc.psum_tensor("m_ps0", [128, 1536], F32))
    psS1 = ctx.enter_context(nc.psum_tensor("m_ps1", [128, 1536], F32))
    psO = [ctx.enter_context(nc.psum_tensor("m_psO0", [128, 512], F32)),
           ctx.enter_context(nc.psum_tensor("m_psO1", [128, 512], F32))]
    Y_PS = [psS0[:, 0:512], psS0[:, 512:1024], psS0[:, 1024:1536],
            psS1[:, 0:512], psS1[:, 512:1024], psS1[:, 1024:1536],
            psO[0][:, :], psO[1][:, :]]

    s_in = [pb.sem("m_in0"), pb.sem("m_in1")]
    s_w = pb.sem("m_w")
    s_w1 = [pb.sem("m_w1a"), pb.sem("m_w1b")]
    s_w2 = [pb.sem("m_w2a"), pb.sem("m_w2b")]
    s_pe = pb.sem("m_pe")
    s_act = pb.sem("m_act")
    s_dve = pb.sem("m_dve")
    s_pool = pb.sem("m_pool")
    s_x2 = [pb.sem("m_x20"), pb.sem("m_x21")]

    E = dict(sq=sq, stt_=stt_, xn_tok=xn_tok, xn_fm=xn_fm, sid=sid, cst=cst,
             psT=psS0, s_act=s_act, s_dve=s_dve, s_pe=s_pe, tok_init=None)

    @block.sync
    def _(sync):
        s_w.inc(sync.dma_start(out=sgate[:, :].rearrange("p (di e) -> p di e", di=8),
                               in_=io["gate"].rearrange("(di p) e -> p di e", p=128)), 16)
        s_w.inc(sync.dma_start(out=b1l[:, :], in_=io["b1l"]), 16)
        s_w.inc(sync.dma_start(out=b2all[:, :], in_=io["b2"]), 16)
        s_w.inc(sync.dma_start(out=sid[:, :], in_=io["ident"]), 16)

    @block.gpsimd
    def _(g):
        s_pool.inc(g.memset(ones[:, :], 1.0))
        s_pool.inc(g.memset(cst[:, 0:1], 1e-5))
        s_pool.inc(g.memset(cst[:, 1:2], 1e-6))
    tok_init = s_pool.tok()
    E["tok_init"] = tok_init

    tok_xfree = [[], []]
    for i in range(NTT):
        par = i % 2
        xv = xt[:, par * 4 * D:(par + 1) * 4 * D]

        @block.sync
        def _(sync, i=i, par=par, xv=xv):
            pb.wait(sync, "sync", tok_xfree[par])
            pb.wait(sync, "sync", s_w.tok())
            tok = s_in[par].inc(
                sync.dma_start(
                    out=xv.rearrange("p (s d) -> p s d", s=4),
                    in_=io["x1"][i * TT:(i + 1) * TT, :].rearrange("(s p) d -> p s d", p=128)), 16)
            tok_xload[par] = tok

        tok_xnfm = emit_ln_tile(pb, block, E, xv, tok_xload[par])

        for s in range(4):
            @block.tensor
            def _(t, s=s, tok_xnfm=tok_xnfm):
                pb.wait(t, "pe", tok_xnfm)
                pb.wait(t, "pe", s_w.tok())
                pb.wait(t, "pe", s_dve.tok())
                for di in range(8):
                    ins = t.matmul(psO[1][:, s * 8:(s + 1) * 8],
                                   xn_fm[:, di * TT + s * 128: di * TT + s * 128 + 128],
                                   sgate[:, di * 8:(di + 1) * 8],
                                   start=(di == 0), stop=(di == 7))
                s_pe.inc(ins)
            tok_g = s_pe.tok()

            @block.vector
            def _(v, s=s, tok_g=tok_g):
                pb.wait(v, "dve", tok_g)
                pb.wait(v, "dve", s_dve.tok())
                pg = psO[1][:, s * 8:(s + 1) * 8]
                s_dve.inc(v.reduce_max(gst[:, 0:1], pg, axis=AX.X, negate=True))
            tok_gm = s_dve.tok()

            @block.scalar
            def _(sc, s=s, tok_gm=tok_gm):
                pb.wait(sc, "act", tok_gm)
                pb.wait(sc, "act", s_act.tok())
                pg = psO[1][:, s * 8:(s + 1) * 8]
                s_act.inc(sc.activation(gex[:, 0:8], pg, AF.Exp, bias=gst[:, 0:1],
                                        accum_out=gst[:, 1:2]))

            @block.vector
            def _(v, s=s):
                pb.wait(v, "dve", s_act.tok())
                s_dve.inc(v.reciprocal(gst[:, 2:3], gst[:, 1:2]))
                pb.wait(v, "dve", s_dve.tok())
                s_dve.inc(v.tensor_scalar_mul(gex[:, 8:16], gex[:, 0:8], gst[:, 2:3]))
                pb.wait(v, "dve", s_dve.tok())
                s_dve.inc(v.reduce_max(gst[:, 3:4], gex[:, 8:16], axis=AX.X))
                pb.wait(v, "dve", s_dve.tok())
                s_dve.inc(v.tensor_single_scalar(gex[:, 16:24], gex[:, 8:16], gst[:, 3:4], ALU.is_ge))
                pb.wait(v, "dve", s_dve.tok())
                s_dve.inc(v.scalar_tensor_tensor(gex[:, 16:24], gex[:, 16:24], NEG,
                                                 gex[:, 8:16], ALU.mult, ALU.add))
                pb.wait(v, "dve", s_dve.tok())
                s_dve.inc(v.reduce_max(gst[:, 4:5], gex[:, 16:24], axis=AX.X))
                pb.wait(v, "dve", s_dve.tok())
                s_dve.inc(v.tensor_single_scalar(gex[:, 16:24], gex[:, 8:16], gst[:, 4:5], ALU.is_ge))
                pb.wait(v, "dve", s_dve.tok())
                s_dve.inc(v.tensor_mul(gex[:, 16:24], gex[:, 16:24], gex[:, 8:16]))
                pb.wait(v, "dve", s_dve.tok())
                s_dve.inc(v.reduce_sum(gst[:, 5:6], gex[:, 16:24], axis=AX.X))
                pb.wait(v, "dve", s_dve.tok())
                s_dve.inc(v.tensor_scalar_max(gst[:, 6:7], gst[:, 5:6], 1e-12))
                pb.wait(v, "dve", s_dve.tok())
                s_dve.inc(v.reciprocal(gst[:, 7:8], gst[:, 6:7]))
                pb.wait(v, "dve", s_dve.tok())
                s_dve.inc(v.tensor_scalar_mul(gcol[:, s * 8:(s + 1) * 8], gex[:, 16:24], gst[:, 7:8]))
            tok_gates = s_dve.tok()

            @block.tensor
            def _(t, s=s, tok_gates=tok_gates):
                pb.wait(t, "pe", tok_gates)
                pb.wait(t, "pe", s_act.tok())
                pb.wait(t, "pe", s_dve.tok())
                pb.wait(t, "pe", s_dve.tok())
                ins = t.transpose(psO[0][0:8, s * 128:(s + 1) * 128],
                                  gcol[:, s * 8:(s + 1) * 8], sid[:, :])
                s_pe.inc(ins)
            tok_gt = s_pe.tok()

            @block.scalar
            def _(sc, s=s, tok_gt=tok_gt):
                pb.wait(sc, "act", tok_gt)
                pb.wait(sc, "act", s_pe.tok())
                s_act.inc(sc.activation(gT[0:8, s * 128:(s + 1) * 128],
                                        psO[0][0:8, s * 128:(s + 1) * 128], AF.Identity))
        tok_gT = s_act.tok()
        tok_gates_all = s_dve.tok()

        tok_ev_prev_y = None
        for e in range(8):
            for dh in range(16):
                wpar = dh % 2
                wslc = wb1[:, wpar * 1024:(wpar + 1) * 1024]

                @block.sync
                def _(sync, e=e, dh=dh, wpar=wpar, wslc=wslc):
                    pb.wait(sync, "sync", (s_pe, max(0, s_pe.n - 1)) if (e + dh) else (s_pe, 0))
                    s_w1[wpar].inc(sync.dma_start(
                        out=wslc.rearrange("p (di j) -> p di j", di=8),
                        in_=io["W1"][e].rearrange("(di p) (dhc j) -> p di dhc j",
                                                  p=128, j=128)[:, :, dh, :]), 16)
                tok_w1 = s_w1[wpar].tok()

                @block.tensor
                def _(t, e=e, dh=dh, wslc=wslc, tok_w1=tok_w1, tok_xnfm=tok_xnfm):
                    pb.wait(t, "pe", tok_w1)
                    pb.wait(t, "pe", tok_xnfm)
                    pb.wait(t, "pe", s_act.tok())
                    pb.wait(t, "pe", s_dve.tok())
                    if dh == 0:
                        pb.wait(t, "pe", s_dve.tok())
                    ps = psS1[:, (dh % 3) * 512:((dh % 3) + 1) * 512]
                    for di in range(8):
                        ins = t.matmul(ps, wslc[:, di * 128:(di + 1) * 128],
                                       xn_fm[:, di * TT:(di + 1) * TT],
                                       start=(di == 0), stop=(di == 7))
                    s_pe.inc(ins)
                tok_h = s_pe.tok()

                @block.scalar
                def _(sc, e=e, dh=dh, tok_h=tok_h):
                    pb.wait(sc, "act", tok_h)
                    pb.wait(sc, "act", s_pe.tok())
                    ps = psS1[:, (dh % 3) * 512:((dh % 3) + 1) * 512]
                    s_act.inc(sc.activation(hb[:, dh * TT:(dh + 1) * TT], ps, AF.Relu,
                                            bias=b1l[:, e * 16 + dh:e * 16 + dh + 1]))
            tok_hb = s_act.tok()

            for dh in range(16):
                wpar = dh % 2
                w2slc = wb2[:, wpar * 1024:(wpar + 1) * 1024]

                @block.sync
                def _(sync, e=e, dh=dh, wpar=wpar, w2slc=w2slc):
                    pb.wait(sync, "sync", (s_pe, max(0, s_pe.n - 1)))
                    s_w2[wpar].inc(sync.dma_start(
                        out=w2slc, in_=io["W2"][e][dh * 128:(dh + 1) * 128, :]), 16)
                tok_w2 = s_w2[wpar].tok()

                @block.tensor
                def _(t, e=e, dh=dh, w2slc=w2slc, tok_w2=tok_w2, tok_hb=tok_hb):
                    pb.wait(t, "pe", tok_w2)
                    pb.wait(t, "pe", tok_hb)
                    if dh == 0:
                        pb.wait(t, "pe", s_dve.tok())
                    for s in range(4):
                        for half in range(2):
                            ins = t.matmul(Y_PS[s * 2 + half],
                                           hb[:, dh * TT + s * 128: dh * TT + s * 128 + 128],
                                           w2slc[:, half * 512:(half + 1) * 512],
                                           start=(dh == 0), stop=(dh == 15))
                    s_pe.inc(ins)
            tok_y = s_pe.tok()

            @block.vector
            def _(v, e=e, i=i, tok_y=tok_y, tok_gates_all=tok_gates_all):
                pb.wait(v, "dve", tok_y)
                pb.wait(v, "dve", tok_gates_all)
                pb.wait(v, "dve", s_dve.tok())
                for s in range(4):
                    for half in range(2):
                        dst = oacc[:, s * D + half * 512: s * D + (half + 1) * 512]
                        gsc = gcol[:, s * 8 + e:s * 8 + e + 1]
                        if e == 0:
                            ins = v.tensor_scalar(dst, Y_PS[s * 2 + half], gsc, None, ALU.mult)
                        else:
                            ins = v.scalar_tensor_tensor(dst, Y_PS[s * 2 + half], gsc, dst,
                                                         ALU.mult, ALU.add)
                        s_dve.inc(ins)
        for s in range(4):
            for half in range(2):
                @block.tensor
                def _(t, s=s, half=half, tok_gT=tok_gT):
                    pb.wait(t, "pe", tok_gT)
                    pb.wait(t, "pe", s_dve.tok())
                    ins = t.matmul(Y_PS[s * 2 + half], gT[0:8, s * 128:(s + 1) * 128],
                                   b2all[0:8, half * 512:(half + 1) * 512],
                                   start=True, stop=True)
                    s_pe.inc(ins)
                tok_b2 = s_pe.tok()

                @block.vector
                def _(v, s=s, half=half, tok_b2=tok_b2):
                    pb.wait(v, "dve", tok_b2)
                    pb.wait(v, "dve", s_dve.tok())
                    dst = oacc[:, s * D + half * 512: s * D + (half + 1) * 512]
                    s_dve.inc(v.tensor_add(dst, dst, Y_PS[s * 2 + half]))

        for s in range(4):
            sp = (i * 4 + s) % 2

            @block.vector
            def _(v, i=i, s=s, sp=sp, par=par, xv=xv):
                pb.wait(v, "dve", s_dve.tok())
                pb.wait(v, "dve", (s_x2[sp], s_x2[sp].n))
                s_dve.inc(v.tensor_add(x2st[:, sp * D:(sp + 1) * D],
                                       oacc[:, s * D:(s + 1) * D],
                                       xv[:, s * D:(s + 1) * D]))
            tok_x2 = s_dve.tok()

            @block.gpsimd
            def _(g, i=i, s=s, sp=sp, tok_x2=tok_x2):
                pb.wait(g, "pool", tok_x2)
                s_x2[sp].inc(g.dma_start(
                    out=io["x2"][i * TT + s * 128: i * TT + (s + 1) * 128, :],
                    in_=x2st[:, sp * D:(sp + 1) * D]), 16)

        tok_xfree[par] = [s_dve.tok()]

    @block.gpsimd
    def _(g):
        for ss in (s_x2[0], s_x2[1]):
            g.wait_ge(ss.h, ss.n)


def build_moe_program():
    nc = bass.Bass("TRN2", target_bir_lowering=False)
    io = {}
    io["x1"] = nc.dram_tensor("x1", [T, D], F32, kind="ExternalInput").ap()
    io["gate"] = nc.dram_tensor("gate", [D, 8], F32, kind="ExternalInput").ap()
    io["W1"] = nc.dram_tensor("W1", [8, D, 2048], F32, kind="ExternalInput").ap()
    io["b1l"] = nc.dram_tensor("b1l", [128, 128], F32, kind="ExternalInput").ap()
    io["W2"] = nc.dram_tensor("W2", [8, 2048, D], F32, kind="ExternalInput").ap()
    io["b2"] = nc.dram_tensor("b2", [8, D], F32, kind="ExternalInput").ap()
    io["ident"] = nc.dram_tensor("ident", [128, 128], F32, kind="ExternalInput").ap()
    io["x2"] = nc.dram_tensor("x2", [T, D], F32, kind="ExternalOutput").ap()
    pb = PB(nc)
    global tok_xload
    tok_xload = [None, None]
    with ExitStack() as ctx:
        block = ctx.enter_context(nc.Block())
        emit_moe(pb, block, io, ctx)
    return nc


def moe_inputs(z):
    C = host_constants()
    b1 = z["L0_ffn_b1"]
    b1l = np.stack([b1[e].reshape(16, 128).T for e in range(8)], 1).reshape(128, 128)
    b1l = np.concatenate([b1[e].reshape(16, 128).T for e in range(8)], axis=1)
    return {
        "gate": np.ascontiguousarray(z["L0_ffn_gate"]),
        "W1": np.ascontiguousarray(z["L0_ffn_W1"]),
        "b1l": np.ascontiguousarray(b1l),
        "W2": np.ascontiguousarray(z["L0_ffn_W2"]),
        "b2": np.ascontiguousarray(z["L0_ffn_b2"]),
        "ident": C["ident"],
    }



def emit_l1f(pb, block, io, ctx):
    nc = pb.nc
    sb = lambda n, sh: ctx.enter_context(nc.sbuf_tensor(n, sh, F32))

    xt = sb("f_xt", [128, 4 * D])
    sq = sb("f_sq", [128, D])
    stt_ = sb("f_st", [128, 64])
    xn_tok = sb("f_xntok", [128, D])
    xn_fm = sb("f_xnfm", [128, NCD * TT])
    gw1 = sb("f_gw1", [128, 2 * 1024])
    gw2 = sb("f_gw2", [128, 64])
    gb1l = sb("f_gb1l", [128, 8])
    gb2r = sb("f_gb2r", [1, 8])
    hg = sb("f_hg", [128, TT])
    hbar = sb("f_hbar", [128, 16])
    grow = sb("f_grow", [1, 16])
    gbc = sb("f_gbc", [128, 8])
    b1li = sb("f_b1li", [128, 32])
    tb1l = sb("f_tb1l", [128, 8 * 32])
    mb1 = sb("f_mb1", [128, 32])
    b2r = sb("f_b2r", [1, D])
    tb2r = sb("f_tb2r", [1, D])
    mb2 = sb("f_mb2", [1, D])
    wtl = sb("f_wtl", [128, 512])
    ttl = sb("f_ttl", [128, 2 * 8 * 512])
    macc = sb("f_macc", [128, 2 * 512])
    ub = sb("f_ub", [128, 32 * TT])
    w1s = sb("f_w1s", [128, 2 * 1024])
    w2s = sb("f_w2s", [128, 2 * 1024])
    x3s = sb("f_x3s", [128, 2 * D])
    ost = sb("f_ost", [128, 2 * D])
    ones = sb("f_ones", [1, 128])
    sid = sb("f_sid", [128, 128])
    cst = sb("f_cst", [128, 8])

    psS0 = ctx.enter_context(nc.psum_tensor("f_ps0", [128, 1536], F32))
    psS1 = ctx.enter_context(nc.psum_tensor("f_ps1", [128, 1536], F32))
    psO = [ctx.enter_context(nc.psum_tensor("f_psO0", [128, 512], F32)),
           ctx.enter_context(nc.psum_tensor("f_psO1", [128, 512], F32))]
    Y_PS = [psS0[:, 0:512], psS0[:, 512:1024], psS0[:, 1024:1536],
            psS1[:, 0:512], psS1[:, 512:1024], psS1[:, 1024:1536],
            psO[0][:, :], psO[1][:, :]]

    s_in = [pb.sem("f_in0"), pb.sem("f_in1")]
    s_w = pb.sem("f_w")
    s_pe = pb.sem("f_pe")
    s_act = pb.sem("f_act")
    s_dve = pb.sem("f_dve")
    s_pool = pb.sem("f_pool")
    s_xn = [pb.sem("f_xn0"), pb.sem("f_xn1")]
    s_g1 = [pb.sem("f_g10"), pb.sem("f_g11")]
    s_m = [pb.sem("f_m0"), pb.sem("f_m1")]
    s_mw = [pb.sem("f_mw0"), pb.sem("f_mw1")]
    s_mt = [pb.sem("f_mt0"), pb.sem("f_mt1")]
    s_u1 = [pb.sem("f_u10"), pb.sem("f_u11")]
    s_u2 = [pb.sem("f_u20"), pb.sem("f_u21")]
    s_xf = [pb.sem("f_xf0"), pb.sem("f_xf1")]
    s_x3 = [pb.sem("f_x30"), pb.sem("f_x31")]
    s_o = [pb.sem("f_o0"), pb.sem("f_o1")]

    E = dict(sq=sq, stt_=stt_, xn_tok=xn_tok, xn_fm=xn_fm, sid=sid, cst=cst,
             psT=psS0, s_act=s_act, s_dve=s_dve, s_pe=s_pe, tok_init=None)

    @block.sync
    def _(sync):
        s_w.inc(sync.dma_start(out=gw2[:, :].rearrange("p (di e) -> p di e", di=8),
                               in_=io["gw2T"].rearrange("(di p) e -> p di e", p=128)), 16)
        s_w.inc(sync.dma_start(out=gb1l[:, :], in_=io["gb1l"]), 16)
        s_w.inc(sync.dma_start(out=gb2r[:, :], in_=io["gb2r"]), 16)
        s_w.inc(sync.dma_start(out=b1li[:, :], in_=io["b1l"]), 16)
        s_w.inc(sync.dma_start(out=tb1l[:, :].rearrange("p (e j) -> p e j", e=8),
                               in_=io["Tb1l"].rearrange("e p j -> p e j")), 16)
        s_w.inc(sync.dma_start(out=b2r[:, :], in_=io["b2r"]), 16)
        s_w.inc(sync.dma_start(out=sid[:, :], in_=io["ident"]), 16)

    @block.gpsimd
    def _(g):
        s_pool.inc(g.memset(ones[:, :], 1.0))
        s_pool.inc(g.memset(cst[:, 0:1], 1e-5))
        s_pool.inc(g.memset(hbar[:, :], 0.0))
    tok_init = s_pool.tok()
    E["tok_init"] = tok_init

    tok_xfree = [[], []]
    for i in range(NTT):
        par = 0
        xv = xt[:, 0:4 * D]

        @block.sync
        def _(sync, i=i, par=par, xv=xv):
            pb.wait(sync, "sync", tok_xfree[par])
            pb.wait(sync, "sync", s_w.tok())
            tok_xload[par] = s_in[par].inc(
                sync.dma_start(
                    out=xv.rearrange("p (s d) -> p s d", s=4),
                    in_=io["x3"][i * TT:(i + 1) * TT, :].rearrange("(s p) d -> p s d", p=128)), 16)

        E["xnfm_wait"] = [(s_xn[0], s_xn[0].n), (s_xn[1], s_xn[1].n)]
        tok_xnfm = emit_ln_tile(pb, block, E, xv, tok_xload[par])

        @block.gpsimd
        def _(g, i=i, tok_xnfm=tok_xnfm):
            pb.wait(g, "pool", tok_xnfm)
            s_xn[i % 2].inc(g.dma_start(
                out=io["xnfm"].rearrange("(c p) t -> p c t", p=128)[:, :, i * TT:(i + 1) * TT],
                in_=xn_fm[:, :].rearrange("p (c t) -> p c t", c=8)), 16)

        for do in range(8):
            gp = do % 2

            @block.sync
            def _(sync, i=i, do=do, gp=gp):
                pb.wait(sync, "sync", (s_pe, max(0, s_pe.n - 1)))
                s_g1[gp].inc(sync.dma_start(
                    out=gw1[:, gp * 1024:(gp + 1) * 1024].rearrange("p (di j) -> p di j", di=8),
                    in_=io["gw1T"].rearrange("(di p) (dc j) -> p di dc j",
                                             p=128, j=128)[:, :, do, :]), 16)
            tok_gw = s_g1[gp].tok()

            @block.tensor
            def _(t, do=do, gp=gp, tok_gw=tok_gw, tok_xnfm=tok_xnfm):
                pb.wait(t, "pe", tok_gw)
                pb.wait(t, "pe", tok_xnfm)
                pb.wait(t, "pe", s_act.tok())
                ps = psS1[:, (do % 3) * 512:((do % 3) + 1) * 512]
                for di in range(8):
                    ins = t.matmul(ps, gw1[:, gp * 1024 + di * 128: gp * 1024 + (di + 1) * 128],
                                   xn_fm[:, di * TT:(di + 1) * TT],
                                   start=(di == 0), stop=(di == 7))
                s_pe.inc(ins)
            tok_hm = s_pe.tok()

            @block.scalar
            def _(sc, do=do, tok_hm=tok_hm):
                pb.wait(sc, "act", tok_hm)
                pb.wait(sc, "act", s_dve.tok())
                ps = psS1[:, (do % 3) * 512:((do % 3) + 1) * 512]
                s_act.inc(sc.activation(hg[:, :], ps, AF.Relu, bias=gb1l[:, do:do + 1]))

            @block.vector
            def _(v, do=do):
                pb.wait(v, "dve", s_act.tok())
                pb.wait(v, "dve", s_dve.tok())
                pb.wait(v, "dve", tok_init)
                s_dve.inc(v.reduce_sum(stt_[:, 34:35], hg[:, :], axis=AX.X))
                pb.wait(v, "dve", s_dve.tok())
                s_dve.inc(v.tensor_add(hbar[:, do:do + 1], hbar[:, do:do + 1], stt_[:, 34:35]))
        tok_xfree[par] = [s_dve.tok(), s_act.tok()]

    tok_hbar = s_dve.tok()

    @block.vector
    def _(v):
        pb.wait(v, "dve", tok_hbar)
        s_dve.inc(v.tensor_scalar_mul(hbar[:, 8:16], hbar[:, 0:8], 1.0 / T))
    tok_hm2 = s_dve.tok()

    @block.tensor
    def _(t):
        pb.wait(t, "pe", tok_hm2)
        pb.wait(t, "pe", s_w.tok())
        for c in range(8):
            t.matmul(psO[0][0:1, 0:8], hbar[:, 8 + c:9 + c], gw2[:, c * 8:(c + 1) * 8],
                     start=(c == 0), stop=False)
        ins = t.matmul(psO[0][0:1, 0:8], ones[0:1, 0:1], gb2r[0:1, :], start=False, stop=True)
        s_pe.inc(ins)
    tok_gps = s_pe.tok()

    @block.scalar
    def _(sc):
        pb.wait(sc, "act", tok_gps)
        s_act.inc(sc.activation(grow[0:1, 0:8], psO[0][0:1, 0:8], AF.Identity))
    tok_grow = s_act.tok()

    @block.tensor
    def _(t):
        pb.wait(t, "pe", tok_grow)
        ins = t.matmul(psO[0][:, 16:24], ones[0:1, :], grow[0:1, 0:8], start=True, stop=True)
        s_pe.inc(ins)
    tok_gbc = s_pe.tok()

    @block.scalar
    def _(sc):
        pb.wait(sc, "act", tok_gbc)
        s_act.inc(sc.activation(gbc[:, :], psO[0][:, 16:24], AF.Identity))
    tok_gbc2 = s_act.tok()

    @block.vector
    def _(v):
        pb.wait(v, "dve", tok_gbc2)
        pb.wait(v, "dve", s_dve.tok())
        s_dve.inc(v.tensor_copy(mb1[:, :], b1li[:, :]))
        for e in range(8):
            pb.wait(v, "dve", s_dve.tok())
            s_dve.inc(v.scalar_tensor_tensor(mb1[:, :], tb1l[:, e * 32:(e + 1) * 32],
                                             gbc[:, e:e + 1], mb1[:, :], ALU.mult, ALU.add))
        pb.wait(v, "dve", s_dve.tok())
        s_dve.inc(v.tensor_copy(mb2[:, :], b2r[:, :]))

    for e in range(8):
        @block.sync
        def _(sync, e=e):
            pb.wait(sync, "sync", (s_dve, s_dve.n))
            s_in[1].inc(sync.dma_start(out=tb2r[:, :], in_=io["Tb2r"][0:1, e * D:(e + 1) * D]), 16)
        tok_t2 = s_in[1].tok()

        @block.vector
        def _(v, e=e, tok_t2=tok_t2):
            pb.wait(v, "dve", tok_t2)
            pb.wait(v, "dve", s_dve.tok())
            s_dve.inc(v.scalar_tensor_tensor(mb2[:, :], tb2r[0:1, :],
                                             grow[0:1, e:e + 1], mb2[:, :], ALU.mult, ALU.add))
    tok_mb = s_dve.tok()

    def mod_stream(wT, twT, outT, rows, cols, base):
        ntile = (rows // 128) * (cols // 512)
        for ti in range(ntile):
            rc = ti // (cols // 512)
            fc = ti % (cols // 512)
            mp = ti % 2

            @block.sync
            def _(sync, rc=rc, fc=fc, mp=mp, wT=wT, twT=twT):
                pb.wait(sync, "sync", (s_dve, max(0, s_dve.n)))
                s_mw[mp].inc(sync.dma_start(
                    out=wtl[:, 0:512],
                    in_=wT[rc * 128:(rc + 1) * 128, fc * 512:(fc + 1) * 512]), 16)
                s_mt[mp].inc(sync.dma_start(
                    out=ttl[:, mp * 8 * 512:(mp + 1) * 8 * 512].rearrange("p (e j) -> p e j", e=8),
                    in_=twT[:, rc * 128:(rc + 1) * 128, fc * 512:(fc + 1) * 512].rearrange(
                        "e p j -> p e j")), 16)
            tok_wld = [(s_mw[mp], s_mw[mp].n), (s_mt[mp], s_mt[mp].n)]

            @block.vector
            def _(v, mp=mp, tok_wld=tok_wld):
                pb.wait(v, "dve", tok_wld)
                pb.wait(v, "dve", tok_mb)
                pb.wait(v, "dve", (s_m[mp], s_m[mp].n))
                dst = macc[:, mp * 512:(mp + 1) * 512]
                tt = ttl[:, mp * 8 * 512:(mp + 1) * 8 * 512]
                s_dve.inc(v.scalar_tensor_tensor(dst, tt[:, 0:512], gbc[:, 0:1],
                                                 wtl[:, 0:512], ALU.mult, ALU.add))
                for e in range(1, 8):
                    pb.wait(v, "dve", s_dve.tok())
                    s_dve.inc(v.scalar_tensor_tensor(dst, tt[:, e * 512:(e + 1) * 512],
                                                     gbc[:, e:e + 1], dst, ALU.mult, ALU.add))
            tok_acc = s_dve.tok()

            @block.gpsimd
            def _(g, rc=rc, fc=fc, mp=mp, outT=outT, tok_acc=tok_acc):
                pb.wait(g, "pool", tok_acc)
                s_m[mp].inc(g.dma_start(
                    out=outT[rc * 128:(rc + 1) * 128, fc * 512:(fc + 1) * 512],
                    in_=macc[:, mp * 512:(mp + 1) * 512]), 16)

    mod_stream(io["W1T"], io["TW1T"], io["mW1T"], 1024, 4096, 0)
    mod_stream(io["W2T"], io["TW2T"], io["mW2T"], 4096, 1024, 0)
    tok_mod_done = [(s_m[0], s_m[0].n), (s_m[1], s_m[1].n)]

    for i in range(NTT):
        par = i % 2

        @block.sync
        def _(sync, i=i, par=par):
            pb.wait(sync, "sync", s_pe.tok())
            pb.wait(sync, "sync", (s_xn[0], s_xn[0].n))
            pb.wait(sync, "sync", (s_xn[1], s_xn[1].n))
            s_xf[par].inc(sync.dma_start(
                out=xn_fm[:, :].rearrange("p (c t) -> p c t", c=8),
                in_=io["xnfm"].rearrange("(c p) t -> p c t", p=128)[:, :, i * TT:(i + 1) * TT]), 16)
        tok_xf = s_xf[par].tok()

        for fc in range(32):
            mp = fc % 2

            @block.sync
            def _(sync, i=i, fc=fc, mp=mp):
                pb.wait(sync, "sync", tok_mod_done)
                pb.wait(sync, "sync", (s_pe, max(0, s_pe.n - 1)))
                s_u1[mp].inc(sync.dma_start(
                    out=w1s[:, mp * 1024:(mp + 1) * 1024].rearrange("p (hc j) -> p hc j", hc=8),
                    in_=io["mW1T"].rearrange("(hc p) f -> p hc f", p=128)[:, :, fc * 128:(fc + 1) * 128]), 16)
            tok_w1 = s_u1[mp].tok()

            @block.tensor
            def _(t, fc=fc, mp=mp, tok_w1=tok_w1, tok_xf=tok_xf):
                pb.wait(t, "pe", tok_w1)
                pb.wait(t, "pe", tok_xf)
                pb.wait(t, "pe", s_act.tok())
                ps = psS1[:, (fc % 3) * 512:((fc % 3) + 1) * 512]
                for hc in range(8):
                    ins = t.matmul(ps, w1s[:, mp * 1024 + hc * 128: mp * 1024 + (hc + 1) * 128],
                                   xn_fm[:, hc * TT:(hc + 1) * TT],
                                   start=(hc == 0), stop=(hc == 7))
                s_pe.inc(ins)
            tok_u = s_pe.tok()

            @block.scalar
            def _(sc, fc=fc, tok_u=tok_u):
                pb.wait(sc, "act", tok_u)
                pb.wait(sc, "act", s_pe.tok())
                pb.wait(sc, "act", s_dve.tok())
                ps = psS1[:, (fc % 3) * 512:((fc % 3) + 1) * 512]
                s_act.inc(sc.activation(ub[:, fc * TT:(fc + 1) * TT], ps, AF.Gelu_apprx_tanh,
                                        bias=mb1[:, fc:fc + 1]))
        tok_ub = s_act.tok()

        for fc in range(32):
            mp = fc % 2

            @block.sync
            def _(sync, fc=fc, mp=mp):
                pb.wait(sync, "sync", (s_pe, max(0, s_pe.n - 1)))
                s_u2[mp].inc(sync.dma_start(
                    out=w2s[:, mp * 1024:(mp + 1) * 1024],
                    in_=io["mW2T"][fc * 128:(fc + 1) * 128, :]), 16)
            tok_w2 = s_u2[mp].tok()

            @block.tensor
            def _(t, fc=fc, mp=mp, tok_w2=tok_w2, tok_ub=tok_ub):
                pb.wait(t, "pe", tok_w2)
                pb.wait(t, "pe", tok_ub)
                if fc == 0:
                    pb.wait(t, "pe", s_dve.tok())
                for s in range(4):
                    for half in range(2):
                        ins = t.matmul(Y_PS[s * 2 + half],
                                       ub[:, fc * TT + s * 128: fc * TT + s * 128 + 128],
                                       w2s[:, mp * 1024 + half * 512: mp * 1024 + (half + 1) * 512],
                                       start=(fc == 0), stop=False)
                s_pe.inc(ins)

        @block.tensor
        def _(t):
            pb.wait(t, "pe", s_dve.tok())
            for s in range(4):
                for half in range(2):
                    ins = t.matmul(Y_PS[s * 2 + half], ones[0:1, 0:128],
                                   mb2[0:1, half * 512:(half + 1) * 512],
                                   start=False, stop=True)
            s_pe.inc(ins)
        tok_y = s_pe.tok()

        for s in range(4):
            sp = (i * 4 + s) % 2

            @block.sync
            def _(sync, i=i, s=s, sp=sp):
                pb.wait(sync, "sync", (s_dve, s_dve.n))
                s_x3[sp].inc(sync.dma_start(
                    out=x3s[:, sp * D:(sp + 1) * D],
                    in_=io["x3"][i * TT + s * 128: i * TT + (s + 1) * 128, :]), 16)
            tok_x3 = s_x3[sp].tok()

            @block.vector
            def _(v, s=s, sp=sp, tok_y=tok_y, tok_x3=tok_x3):
                pb.wait(v, "dve", tok_y)
                pb.wait(v, "dve", tok_x3)
                pb.wait(v, "dve", (s_o[sp], s_o[sp].n))
                for half in range(2):
                    ins = v.tensor_add(ost[:, sp * D + half * 512: sp * D + (half + 1) * 512],
                                       Y_PS[s * 2 + half],
                                       x3s[:, sp * D + half * 512: sp * D + (half + 1) * 512])
                    s_dve.inc(ins)
            tok_o = s_dve.tok()

            @block.gpsimd
            def _(g, i=i, s=s, sp=sp, tok_o=tok_o):
                pb.wait(g, "pool", tok_o)
                s_o[sp].inc(g.dma_start(
                    out=io["out"][i * TT + s * 128: i * TT + (s + 1) * 128, :],
                    in_=ost[:, sp * D:(sp + 1) * D]), 16)

    @block.gpsimd
    def _(g):
        for ss in (s_o[0], s_o[1]):
            g.wait_ge(ss.h, ss.n)


def build_l1aproj_program():
    nc = bass.Bass("TRN2", target_bir_lowering=False)
    io = {}
    io["x"] = nc.dram_tensor("x", [T, D], F32, kind="ExternalInput").ap()
    for n in ("l0qT", "l0kT", "l0vT"):
        io[n] = nc.dram_tensor(n, [D, D], F32, kind="ExternalInput").ap()
    io["bq_l"] = nc.dram_tensor("bq_l", [128, 8], F32, kind="ExternalInput").ap()
    io["bk_l"] = nc.dram_tensor("bk_l", [128, 8], F32, kind="ExternalInput").ap()
    io["bv_row"] = nc.dram_tensor("bv_row", [1, D], F32, kind="ExternalInput").ap()
    io["rel_bias"] = nc.dram_tensor("rel_bias", [16, 257], F32, kind="ExternalInput").ap()
    io["ident"] = nc.dram_tensor("ident", [128, 128], F32, kind="ExternalInput").ap()
    io["cosfm"] = nc.dram_tensor("cosfm", [128, T], F32, kind="ExternalInput").ap()
    io["sinfm"] = nc.dram_tensor("sinfm", [128, T], F32, kind="ExternalInput").ap()
    io["swp"] = nc.dram_tensor("swp", [128, 128], F32, kind="ExternalInput").ap()
    io["Qfm"] = nc.dram_tensor("Qfm", [D, T], F32, kind="ExternalOutput").ap()
    io["Kfm"] = nc.dram_tensor("Kfm", [D, T], F32, kind="ExternalOutput").ap()
    io["Vp"] = nc.dram_tensor("Vp", [T, 1056], F32, kind="ExternalOutput").ap()
    io["cvec"] = nc.dram_tensor("cvec", [128, 48], F32, kind="ExternalOutput").ap()
    pb = PB(nc)
    with ExitStack() as ctx:
        block = ctx.enter_context(nc.Block())
        emit_l0a(pb, block, io, ctx, rms=True, rope=True)
    return nc


def build_l1b_program():
    nc = bass.Bass("TRN2", target_bir_lowering=False)
    io = {}
    io["x"] = nc.dram_tensor("x", [T, D], F32, kind="ExternalInput").ap()
    io["Qfm"] = nc.dram_tensor("Qfm", [D, T], F32, kind="ExternalInput").ap()
    io["Kfm"] = nc.dram_tensor("Kfm", [D, T], F32, kind="ExternalInput").ap()
    io["Vp"] = nc.dram_tensor("Vp", [T, 1056], F32, kind="ExternalInput").ap()
    io["cvec"] = nc.dram_tensor("cvec", [128, 48], F32, kind="ExternalInput").ap()
    io["band"] = nc.dram_tensor("band", [NH, 7, 128, TT], F32, kind="ExternalInput").ap()
    io["l0oT"] = nc.dram_tensor("l0oT", [D, D], F32, kind="ExternalInput").ap()
    io["bo_row"] = nc.dram_tensor("bo_row", [1, D], F32, kind="ExternalInput").ap()
    io["Ofm"] = nc.dram_tensor("Ofm", [D, T], F32, kind="Internal").ap()
    io["x1"] = nc.dram_tensor("x1", [T, D], F32, kind="ExternalOutput").ap()
    pb = PB(nc)
    with ExitStack() as ctx:
        block = ctx.enter_context(nc.Block())
        emit_l0b(pb, block, io, ctx, nd=7, dmin=-512, skip_far=True)
    return nc


def build_l1f_program():
    global tok_xload
    tok_xload = [None, None]
    nc = bass.Bass("TRN2", target_bir_lowering=False)
    io = {}
    io["x3"] = nc.dram_tensor("x3", [T, D], F32, kind="ExternalInput").ap()
    io["gw1T"] = nc.dram_tensor("gw1T", [D, D], F32, kind="ExternalInput").ap()
    io["gw2T"] = nc.dram_tensor("gw2T", [D, 8], F32, kind="ExternalInput").ap()
    io["gb1l"] = nc.dram_tensor("gb1l", [128, 8], F32, kind="ExternalInput").ap()
    io["gb2r"] = nc.dram_tensor("gb2r", [1, 8], F32, kind="ExternalInput").ap()
    io["b1l"] = nc.dram_tensor("b1l", [128, 32], F32, kind="ExternalInput").ap()
    io["Tb1l"] = nc.dram_tensor("Tb1l", [8, 128, 32], F32, kind="ExternalInput").ap()
    io["b2r"] = nc.dram_tensor("b2r", [1, D], F32, kind="ExternalInput").ap()
    io["Tb2r"] = nc.dram_tensor("Tb2r", [1, 8 * D], F32, kind="ExternalInput").ap()
    io["W1T"] = nc.dram_tensor("W1T", [D, 4096], F32, kind="ExternalInput").ap()
    io["TW1T"] = nc.dram_tensor("TW1T", [8, D, 4096], F32, kind="ExternalInput").ap()
    io["W2T"] = nc.dram_tensor("W2T", [4096, D], F32, kind="ExternalInput").ap()
    io["TW2T"] = nc.dram_tensor("TW2T", [8, 4096, D], F32, kind="ExternalInput").ap()
    io["ident"] = nc.dram_tensor("ident", [128, 128], F32, kind="ExternalInput").ap()
    io["xnfm"] = nc.dram_tensor("xnfm", [D, T], F32, kind="Internal").ap()
    io["mW1T"] = nc.dram_tensor("mW1T", [D, 4096], F32, kind="Internal").ap()
    io["mW2T"] = nc.dram_tensor("mW2T", [4096, D], F32, kind="Internal").ap()
    io["out"] = nc.dram_tensor("out", [T, D], F32, kind="ExternalOutput").ap()
    pb = PB(nc)
    with ExitStack() as ctx:
        block = ctx.enter_context(nc.Block())
        emit_l1f(pb, block, io, ctx)
    return nc


def host_l1_band():
    kl = np.arange(128)[:, None]
    ql = np.arange(TT)[None, :]
    out = np.empty((NH, 7, 128, TT), dtype=np.float32)
    for d in range(7):
        delta = -512 + d * 128
        rel = delta + ql - kl
        out[:, d] = np.where((rel >= 0) & (rel <= 256), 0.0, NEG)
    return np.ascontiguousarray(out)


def _swap_matrix():
    P = np.zeros((128, 128), np.float32)
    for j in range(128):
        g = (j // 64) * 64
        P[g + ((j - g) + 32) % 64, j] = 1.0
    return P


def l1a_inputs(z):
    C = host_constants()
    wqkvT = np.ascontiguousarray(z["L1_attn_w_qkv"].T)
    rbf = np.zeros((16, 257), np.float32)
    rbf[:, 0] = NEG
    rbf[:, 256] = NEG
    return {
        "l0qT": np.ascontiguousarray(wqkvT[:, 0:1024]),
        "l0kT": np.ascontiguousarray(wqkvT[:, 1024:2048]),
        "l0vT": np.ascontiguousarray(wqkvT[:, 2048:3072]),
        "bq_l": np.zeros((128, 8), np.float32),
        "bk_l": np.zeros((128, 8), np.float32),
        "bv_row": np.zeros((1, D), np.float32),
        "rel_bias": rbf,
        "ident": C["ident"],
        "cosfm": np.ascontiguousarray(np.tile(C["cosrep"][:, 0:32].T, (4, 1))),
        "sinfm": np.ascontiguousarray(np.concatenate(
            [-C["sinrep"][:, 0:32].T, C["sinrep"][:, 0:32].T] * 2, axis=0)),
        "swp": _swap_matrix(),
        "ident": C["ident"],
    }


def l1b_inputs(z):
    C = host_constants()
    return {
        "band": host_l1_band(),
        "l0oT": np.ascontiguousarray(z["L1_attn_w_out"].T),
        "bo_row": np.zeros((1, D), np.float32),
    }


def l1f_inputs(z):
    C = host_constants()
    return {
        "gw1T": np.ascontiguousarray(z["L1_ffn_gate_w1"].T),
        "gw2T": np.ascontiguousarray(z["L1_ffn_gate_w2"].T),
        "gb1l": np.ascontiguousarray(z["L1_ffn_gate_b1"].reshape(8, 128).T),
        "gb2r": np.ascontiguousarray(z["L1_ffn_gate_b2"].reshape(1, 8)),
        "b1l": np.ascontiguousarray(z["L1_ffn_b1"].reshape(32, 128).T),
        "Tb1l": np.ascontiguousarray(np.stack([z["L1_ffn_Tb1"][e].reshape(32, 128).T for e in range(8)])),
        "b2r": np.ascontiguousarray(z["L1_ffn_b2"].reshape(1, D)),
        "Tb2r": np.ascontiguousarray(z["L1_ffn_Tb2"].reshape(1, 8 * D)),
        "W1T": np.ascontiguousarray(z["L1_ffn_W1"].T),
        "TW1T": np.ascontiguousarray(z["L1_ffn_TW1"].transpose(0, 2, 1)),
        "W2T": np.ascontiguousarray(z["L1_ffn_W2"].T),
        "TW2T": np.ascontiguousarray(z["L1_ffn_TW2"].transpose(0, 2, 1)),
        "ident": C["ident"],
    }



def build_l0a_program():
    nc = bass.Bass("TRN2", target_bir_lowering=False)
    io = {}
    io["x"] = nc.dram_tensor("x", [T, D], F32, kind="ExternalInput").ap()
    for n in ("l0qT", "l0kT", "l0vT"):
        io[n] = nc.dram_tensor(n, [D, D], F32, kind="ExternalInput").ap()
    io["bq_l"] = nc.dram_tensor("bq_l", [128, 8], F32, kind="ExternalInput").ap()
    io["bk_l"] = nc.dram_tensor("bk_l", [128, 8], F32, kind="ExternalInput").ap()
    io["bv_row"] = nc.dram_tensor("bv_row", [1, D], F32, kind="ExternalInput").ap()
    io["rel_bias"] = nc.dram_tensor("rel_bias", [16, 257], F32, kind="ExternalInput").ap()
    io["ident"] = nc.dram_tensor("ident", [128, 128], F32, kind="ExternalInput").ap()
    io["Qfm"] = nc.dram_tensor("Qfm", [D, T], F32, kind="ExternalOutput").ap()
    io["Kfm"] = nc.dram_tensor("Kfm", [D, T], F32, kind="ExternalOutput").ap()
    io["Vp"] = nc.dram_tensor("Vp", [T, 1056], F32, kind="ExternalOutput").ap()
    io["cvec"] = nc.dram_tensor("cvec", [128, 48], F32, kind="ExternalOutput").ap()
    pb = PB(nc)
    with ExitStack() as ctx:
        block = ctx.enter_context(nc.Block())
        emit_l0a(pb, block, io, ctx)
    return nc


def l0a_inputs(x_b, z):
    C = host_constants()
    return {
        "x": np.ascontiguousarray(x_b),
        "l0qT": np.ascontiguousarray(z["L0_attn_Wq"].T),
        "l0kT": np.ascontiguousarray(z["L0_attn_Wk"].T),
        "l0vT": np.ascontiguousarray(z["L0_attn_Wv"].T),
        "bq_l": np.ascontiguousarray(z["L0_attn_bq"].reshape(8, 128).T),
        "bk_l": np.ascontiguousarray(z["L0_attn_bk"].reshape(8, 128).T),
        "bv_row": np.ascontiguousarray(z["L0_attn_bv"].reshape(1, D)),
        "rel_bias": np.ascontiguousarray(z["L0_attn_rel_bias"]),
        "ident": C["ident"],
    }


_PROGS = {}

def _get_prog(name, builder):
    if name not in _PROGS:
        _PROGS[name] = builder()
    return _PROGS[name]


def kernel(**inputs):
    z = inputs
    B = 8
    x_full = np.asarray(z["L0_attn_x"], np.float32)
    im_a = l0a_inputs(x_full[0], z); im_a.pop("x")
    im_b = l0b_inputs(x_full[0], z); im_b.pop("x")
    im_m = moe_inputs(z)
    im_la = l1a_inputs(z)
    im_lb = l1b_inputs(z)
    im_lf = l1f_inputs(z)
    cores = list(range(B))

    nc1 = _get_prog("l0a", build_l0a_program)
    r1 = run_bass_kernel_spmd(nc1, [dict(im_a, x=np.ascontiguousarray(x_full[b])) for b in range(B)], core_ids=cores)
    nc2 = _get_prog("l0b", build_l0b_program)
    r2 = run_bass_kernel_spmd(nc2, [dict(im_b, x=np.ascontiguousarray(x_full[b]),
                                         Qfm=r1.results[b]["Qfm"], Kfm=r1.results[b]["Kfm"],
                                         Vp=r1.results[b]["Vp"], cvec=r1.results[b]["cvec"])
                                    for b in range(B)], core_ids=cores)
    nc3 = _get_prog("moe", build_moe_program)
    r3 = run_bass_kernel_spmd(nc3, [dict(im_m, x1=r2.results[b]["x1"]) for b in range(B)], core_ids=cores)
    nc4 = _get_prog("l1a", build_l1aproj_program)
    r4 = run_bass_kernel_spmd(nc4, [dict(im_la, x=r3.results[b]["x2"]) for b in range(B)], core_ids=cores)
    nc5 = _get_prog("l1b", build_l1b_program)
    r5 = run_bass_kernel_spmd(nc5, [dict(im_lb, x=r3.results[b]["x2"],
                                         Qfm=r4.results[b]["Qfm"], Kfm=r4.results[b]["Kfm"],
                                         Vp=r4.results[b]["Vp"], cvec=r4.results[b]["cvec"])
                                    for b in range(B)], core_ids=cores)
    nc6 = _get_prog("l1f", build_l1f_program)
    r6 = run_bass_kernel_spmd(nc6, [dict(im_lf, x3=r5.results[b]["x1"]) for b in range(B)], core_ids=cores)
    return np.stack([r6.results[b]["out"] for b in range(B)], 0)
